# version 01
# baseline (speedup 1.0000x reference)
import contextlib
import numpy as np
import ml_dtypes
import concourse.bass as bass
import concourse.mybir as mybir
from concourse.bass_utils import run_bass_kernel_spmd

F32 = mybir.dt.float32
BF16 = mybir.dt.bfloat16
I32 = mybir.dt.int32
AF = mybir.ActivationFunctionType
ALU = mybir.AluOpType
AX = mybir.AxisListType

P = 128
TC = 512
TB = TC // P
NCH = 16
CPS = 8
DM = 1024
DMB = DM // P
DI = 2048
DB = DI // P
DFF = 4096
FB = DFF // P
OB = 4096 // P
EB = DM // P
NST = 16
RNK = 64
KC = 4
NQ = 4
DBQ = DB // NQ
TWO_PI = float(2.0 * np.pi)
INV_2PI = float(1.0 / (2.0 * np.pi))


class Plan:
    ENGS = ("sync", "gpsimd", "vector", "scalar", "tensor")

    def __init__(self, nc, stack):
        self.nc = nc
        self.th = {e: [] for e in self.ENGS}
        self.meta = {e: [] for e in self.ENGS}
        self.sems = {}
        self.guards = {}
        self.stack = stack

    def sem(self, name):
        h = self.stack.enter_context(self.nc.semaphore(name))
        self.sems[name] = [h, 0]

    def C(self, name):
        return self.sems[name][1]

    def wait(self, eng, name, val):
        if val <= 0:
            return
        h = self.sems[name][0]
        self.th[eng].append(lambda e, h=h, v=val: e.wait_ge(h, v))
        self.meta[eng].append(("wait", name, val))

    def emit(self, eng, fn, inc=None, k=1):
        if inc is not None:
            h = self.sems[inc][0]
            self.sems[inc][1] += k

            def wrapped(e, fn=fn, h=h, k=k):
                fn(e).then_inc(h, k)
            self.th[eng].append(wrapped)
            self.meta[eng].append(("inc", inc, k))
        else:
            self.th[eng].append(fn)
            self.meta[eng].append(("op", None, 0))

    def mark(self, key, sem):
        self.guards[key] = [(sem, self.C(sem))]

    def also_mark(self, key, sem):
        self.guards.setdefault(key, []).append((sem, self.C(sem)))

    def need(self, eng, key):
        for s, v in self.guards.get(key, ()):
            self.wait(eng, s, v)

    def check(self):
        pos = {e: 0 for e in self.ENGS}
        cnt = {n: 0 for n in self.sems}
        total = sum(len(v) for v in self.meta.values())
        done = 0
        progress = True
        while progress:
            progress = False
            for e in self.ENGS:
                while pos[e] < len(self.meta[e]):
                    kind, name, val = self.meta[e][pos[e]]
                    if kind == "wait" and cnt[name] < val:
                        break
                    if kind == "inc":
                        cnt[name] += val
                    pos[e] += 1
                    done += 1
                    progress = True
        if done != total:
            stuck = {e: (pos[e], len(self.meta[e]),
                         self.meta[e][pos[e]] if pos[e] < len(self.meta[e]) else None)
                     for e in self.ENGS}
            raise RuntimeError(f"plan deadlock: {stuck}")

    def run(self, block):
        for e in self.ENGS:
            ths = self.th[e]

            def body(eng, ths=ths):
                for f in ths:
                    f(eng)
            getattr(block, e)(body)


DEBUG = False
NCH_OVERRIDE = None


def build_program():
    nc = bass.Bass("TRN2", target_bir_lowering=False)
    stack = contextlib.ExitStack()

    def din(name, shape, dt=F32):
        return nc.dram_tensor(name, list(shape), dt, kind="ExternalInput").ap()

    x_in = din("x", (2, 4096, DM))
    w_ip = din("w_ip", (4096, DM))
    w_cv = din("w_cv", (DI, 1, KC))
    b_cv = din("b_cv", (DI,))
    w_xp = din("w_xp", (96, DI))
    w_dt = din("w_dt", (DI, RNK))
    b_dt = din("b_dt", (DI,))
    a_log = din("a_log", (DI, NST))
    d_prm = din("d_prm", (DI,))
    w_op = din("w_op", (DM, DI))
    w_1 = din("w_1", (DM, DFF))
    b_1 = din("b_1", (DFF,))
    q_ph = din("q_ph", (DFF,))
    q_wr = din("q_wr", (DFF,))
    q_wi = din("q_wi", (DFF,))
    w_2 = din("w_2", (DFF, DM))
    b_2 = din("b_2", (DM,))

    out = nc.dram_tensor("out", [2, 4096, DM], F32, kind="ExternalOutput").ap()

    wi_prep = nc.dram_tensor("wi_prep", [OB, P, DMB * P], BF16, kind="Internal").ap()
    w1_prep = nc.dram_tensor("w1_prep", [FB, P, DMB * P], BF16, kind="Internal").ap()
    w2_prep = nc.dram_tensor("w2_prep", [4, FB, P, 256], BF16, kind="Internal").ap()
    wo_prep = nc.dram_tensor("wo_prep", [4, DB, P, 256], BF16, kind="Internal").ap()
    cv_prep = nc.dram_tensor("cv_prep", [DB, P, KC * P], BF16, kind="Internal").ap()
    xs_sp = nc.dram_tensor("xs_sp", [DB, P, NCH * TC], BF16, kind="Internal").ap()
    bc_sp = nc.dram_tensor("bc_sp", [NCH, 32, TC], BF16, kind="Internal").ap()
    sz_sp = nc.dram_tensor("sz_sp", [DB, P, NCH * TC], BF16, kind="Internal").ap()

    dbg = {}
    if DEBUG:
        def dout(name, shape, dt):
            dbg[name] = nc.dram_tensor(name, list(shape), dt, kind="ExternalOutput").ap()
        dout("dbg_xnd", (DMB, P, TC), BF16)
        dout("dbg_stat", (TB, P, 8), F32)
        dout("dbg_xnt", (TB, P, DM), BF16)
        dout("dbg_hid", (P, TC), F32)
        dout("dbg_act", (FB, P, TC), BF16)
        dout("dbg_xdbl", (P, TC), BF16)
        dout("dbg_ndl", (DB, P, TC), F32)
        dout("dbg_wdb", (DB, P, TC), BF16)
        dout("dbg_a", (P, TC), F32)
        dout("dbg_u", (P, TC), BF16)
        dout("dbg_h", (P, TC), BF16)
        dout("dbg_t1", (DB, P, TC), F32)
        dout("dbg_ybf", (DB, P, TC), BF16)
        dout("dbg_comb", (EB, P, TC), BF16)
        dout("dbg_xs", (DB, P, TC), BF16)
        dout("dbg_sz", (DB, P, TC), BF16)

    id_f = nc.inline_tensor(np.eye(P, dtype=np.float32), "id_f").ap()
    id_b = nc.inline_tensor(np.eye(P, dtype=np.float32).astype(ml_dtypes.bfloat16),
                            "id_b").ap()

    def sb(name, shape, dt):
        return stack.enter_context(nc.sbuf_tensor(name, list(shape), dt))

    idf = sb("idf", [P, P], F32)
    idb = sb("idb", [P, P], BF16)
    aexp = sb("aexp", [P, DB * NST], F32)
    dcol = sb("dcol", [P, DB], F32)
    ndtb = sb("ndtb", [P, DB], F32)
    cbc = sb("cbc", [P, DB], F32)
    b1c = sb("b1c", [P, FB], F32)
    b2c = sb("b2c", [P, EB], F32)
    thc = sb("thc", [P, FB], F32)
    thd = sb("thd", [P, FB], F32)
    kqc = sb("kqc", [P, FB], F32)
    epsc = sb("epsc", [P, 1], F32)
    wx = sb("wx", [P, DB * 96], BF16)
    wdt = sb("wdt", [RNK, DB * P], BF16)

    xt = [sb(f"xt{i}", [P, DM], F32) for i in range(2)]
    sq = sb("sq", [P, DM], F32)
    s1b = sq[:, 0:TC]
    s2b = sq[:, TC:DM]
    statc = sb("statc", [P, 8], F32)
    xnt = [sb(f"xnt{i}", [P, DM], BF16) for i in range(4)]
    xnd = [sb(f"xnd{i}", [P, TC], BF16) for i in range(DMB)]
    w1b = sb("w1b", [P, DMB * P], BF16)
    wib = sb("wib", [P, DMB * P], BF16)
    qib = sb("qib", [P, TC], I32)
    qfb = sb("qfb", [P, TC], F32)
    snb = sb("snb", [P, TC], F32)
    hid = [sb(f"hid{i}", [P, TC], F32) for i in range(2)]
    act = [sb(f"act{i}", [P, TC], BF16) for i in range(FB)]
    xpre = [sb(f"xpre{i}", [P, KC - 1 + TC], BF16) for i in range(DB)]
    cvb = sb("cvb", [P, KC * P], BF16)
    szt = sb("szt", [P, TC], BF16)
    xstg = sb("xstg", [P, TC], BF16)
    xdbl = sb("xdbl", [P, TC], BF16)
    sgb = sb("sgb", [P, TC], F32)
    ndl = [sb(f"ndl{i}", [P, TC], F32) for i in range(DBQ)]
    xsl = [sb(f"xsl{i}", [P, TC], BF16) for i in range(DBQ)]
    wdb = [sb(f"wdb{i}", [P, TC], BF16) for i in range(DBQ)]
    szl = [sb(f"szl{i}", [P, TC], BF16) for i in range(2)]
    abuf = [sb(f"abuf{i}", [P, TC], F32) for i in range(2)]
    ubuf = [sb(f"ubuf{i}", [P, TC], BF16) for i in range(2)]
    hbuf = [sb(f"hbuf{i}", [P, TC], BF16) for i in range(2)]
    pbuf = [sb(f"pbuf{i}", [P, TC], BF16) for i in range(2)]
    brep = [sb(f"brep{i}", [P, 2 * TC], BF16) for i in range(2)]
    crep = [sb(f"crep{i}", [P, 2 * TC], BF16) for i in range(2)]
    carry = sb("carry", [P, DB * NST], F32)
    t1b = sb("t1b", [P, TC], F32)
    ybf = [sb(f"ybf{i}", [P, TC], BF16) for i in range(DB)]
    w2b = sb("w2b", [P, 2048], BF16)
    wob = sb("wob", [P, 2048], BF16)
    comb = [sb(f"comb{i}", [P, TC], BF16) for i in range(EB)]
    oute = [sb(f"oute{i}", [P, DM], F32) for i in range(2)]

    PS = stack.enter_context(nc.psum_tensor("PS", [P, 7 * 512], F32))
    PSB = stack.enter_context(nc.psum_tensor("PSB", [P, 1024], BF16))

    def psb(i):
        return PS[:, i * 512:(i + 1) * 512]

    pl = Plan(nc, stack)
    for s in ("qc", "qpw", "qps", "qsp", "sv", "ss", "stt",
              "sx0", "sx1", "sxs0", "sxs1", "sxs2", "sxs3", "ssz0", "ssz1",
              "sw1", "swi", "scv", "sw2", "swo", "sbc0", "sbc1",
              "sszt", "sxst", "sbcs", "so0", "so1"):
        pl.sem(s)
    SY = lambda fn, inc=None, k=16: pl.emit("sync", fn, inc, k)
    GP = lambda fn, inc=None, k=16: pl.emit("gpsimd", fn, inc, k)
    V = lambda fn, inc="sv": pl.emit("vector", fn, inc)
    S = lambda fn, inc="ss": pl.emit("scalar", fn, inc)
    T = lambda fn, inc="stt": pl.emit("tensor", fn, inc)
    W = pl.wait
    C = pl.C
    need = pl.need
    mark = pl.mark
    amark = pl.also_mark

    def dump(cond, dst_ap, src_ap, prod_sem):
        if DEBUG and cond:
            W("gpsimd", prod_sem, C(prod_sem))
            GP(lambda e, d=dst_ap, s=src_ap: e.dma_start(out=d, in_=s), "qps")

    ael = sb("ael", [P, DB * NST], F32)
    wrc = sb("wrc", [P, FB], F32)
    wic = sb("wic", [P, FB], F32)
    cwc = sb("cwc", [P, DB * KC], F32)
    SY(lambda e: e.dma_start(out=idf[:], in_=id_f), "qc")
    SY(lambda e: e.dma_start(out=idb[:], in_=id_b), "qc")
    SY(lambda e: e.dma_start(out=ael[:], in_=bass.AP(a_log.tensor, 0, [[NST, P], [NST * P, DB], [1, NST]])), "qc")
    SY(lambda e: e.dma_start(out=dcol[:], in_=bass.AP(d_prm.tensor, 0, [[1, P], [P, DB]]), allow_slow_non_contiguous=True), "qc")
    SY(lambda e: e.dma_start(out=ndtb[:], in_=bass.AP(b_dt.tensor, 0, [[1, P], [P, DB]]), allow_slow_non_contiguous=True), "qc")
    SY(lambda e: e.dma_start(out=cbc[:], in_=bass.AP(b_cv.tensor, 0, [[1, P], [P, DB]]), allow_slow_non_contiguous=True), "qc")
    SY(lambda e: e.dma_start(out=b1c[:], in_=bass.AP(b_1.tensor, 0, [[1, P], [P, FB]]), allow_slow_non_contiguous=True), "qc")
    SY(lambda e: e.dma_start(out=b2c[:], in_=bass.AP(b_2.tensor, 0, [[1, P], [P, EB]]), allow_slow_non_contiguous=True), "qc")
    SY(lambda e: e.dma_start(out=thc[:], in_=bass.AP(q_ph.tensor, 0, [[1, P], [P, FB]]), allow_slow_non_contiguous=True), "qc")
    SY(lambda e: e.dma_start(out=wrc[:], in_=bass.AP(q_wr.tensor, 0, [[1, P], [P, FB]]), allow_slow_non_contiguous=True), "qc")
    SY(lambda e: e.dma_start(out=wic[:], in_=bass.AP(q_wi.tensor, 0, [[1, P], [P, FB]]), allow_slow_non_contiguous=True), "qc")
    SY(lambda e: e.dma_start(out=cwc[:], in_=bass.AP(w_cv.tensor, 0, [[KC, P], [KC * P, DB], [1, KC]])), "qc")
    W("scalar", "qc", C("qc"))
    W("vector", "qc", C("qc"))
    S(lambda e: e.activation(aexp[:], ael[:], AF.Exp))
    V(lambda e: e.tensor_scalar(ndtb[:], ndtb[:], -1.0, None, ALU.mult))
    V(lambda e: e.memset(epsc[:], 1e-5))
    V(lambda e: e.tensor_scalar(thd[:], thc[:], INV_2PI, None, ALU.mult))
    trg = sb("trg", [P, FB], F32)
    tqi = sb("tqi", [P, FB], I32)
    tqf = sb("tqf", [P, FB], F32)
    costh = sb("costh", [P, FB], F32)
    sinth = sb("sinth", [P, FB], F32)
    for tgt, shift in ((sinth, 0.0), (costh, float(np.pi / 2))):
        V(lambda e, s=shift: e.tensor_scalar(trg[:], thc[:], 1.0, s, ALU.mult, ALU.add))
        V(lambda e: e.drain(), inc=None)
        V(lambda e: e.tensor_scalar(tqf[:], trg[:], INV_2PI, None, ALU.mult))
        V(lambda e: e.drain(), inc=None)
        V(lambda e: e.tensor_copy(tqi[:], tqf[:]))
        V(lambda e: e.drain(), inc=None)
        V(lambda e: e.tensor_copy(tqf[:], tqi[:]))
        V(lambda e: e.drain(), inc=None)
        V(lambda e: e.scalar_tensor_tensor(trg[:], tqf[:], -TWO_PI, trg[:], ALU.mult, ALU.add))
        V(lambda e: e.drain(), inc=None)
        W("scalar", "sv", C("sv"))
        S(lambda e, t=tgt: e.activation(t[:], trg[:], AF.Sin))
        W("vector", "ss", C("ss"))
    V(lambda e: e.tensor_mul(costh[:], costh[:], wrc[:]))
    V(lambda e: e.tensor_mul(sinth[:], sinth[:], wic[:]))
    V(lambda e: e.drain(), inc=None)
    V(lambda e: e.tensor_add(costh[:], costh[:], sinth[:]))
    V(lambda e: e.drain(), inc=None)
    V(lambda e: e.tensor_mul(kqc[:], costh[:], wic[:]))
    V(lambda e: e.drain(), inc=None)
    V(lambda e: e.tensor_scalar(kqc[:], kqc[:], 0.1, None, ALU.mult))
    V(lambda e: e.drain(), inc=None)

    dgb = sb("dgb", [P, P], BF16)
    for dblk in range(DB):
        for k in range(KC):
            W("vector", "qps", C("qps"))
            V(lambda e, d=dblk, k=k: e.tensor_scalar(
                dgb[:], idf[:], cwc[:, d * KC + k:d * KC + k + 1], None, ALU.mult))
            W("gpsimd", "sv", C("sv"))
            GP(lambda e, d=dblk, k=k: e.dma_start(
                out=cv_prep[d][:, k * P:(k + 1) * P], in_=dgb[:]), "qps")

    stg = sb("stg", [P, 1024], F32)
    stgb = sb("stgb", [P, 4096], BF16)
    stg2 = sb("stg2", [P, 1024], BF16)

    def prep_load(src_ap, width, rows=P):
        W("sync", "qps", C("qps"))
        W("vector", "stt", C("stt"))
        for off in range(0, width, 1024):
            cw = min(1024, width - off)
            W("sync", "sv", C("sv"))
            SY(lambda e, a=src_ap, o=off, cw=cw, r=rows: e.dma_start(
                out=stg[0:r, 0:cw], in_=a[:, o:o + cw]), "qpw")
            W("vector", "qpw", C("qpw"))
            V(lambda e, o=off, cw=cw, r=rows: e.tensor_copy(
                stgb[0:r, o:o + cw], stg[0:r, 0:cw]))

    for o in range(OB):
        prep_load(w_ip[o * P:(o + 1) * P, :], DM)
        W("tensor", "sv", C("sv"))
        W("tensor", "ss", C("ss"))
        for dm in range(DMB):
            T(lambda e, dm=dm: e.transpose(PSB[:, dm * P:(dm + 1) * P],
                                           stgb[:, dm * P:(dm + 1) * P], idb[:]),
              "stt" if dm == DMB - 1 else None)
        W("scalar", "stt", C("stt"))
        W("scalar", "qps", C("qps"))
        S(lambda e: e.activation(stg2[:], PSB[:], AF.Identity))
        W("gpsimd", "ss", C("ss"))
        GP(lambda e, o=o: e.dma_start(out=wi_prep[o], in_=stg2[:]), "qps")

    for dm in range(DMB):
        prep_load(w_1[dm * P:(dm + 1) * P, :], DFF)
        W("gpsimd", "sv", C("sv"))
        for f in range(FB):
            GP(lambda e, dm=dm, f=f: e.dma_start(
                out=w1_prep[f][:, dm * P:(dm + 1) * P],
                in_=stgb[:, f * P:(f + 1) * P]), "qps")

    for fi in range(FB):
        prep_load(w_2[fi * P:(fi + 1) * P, :], DM)
        W("gpsimd", "sv", C("sv"))
        for ep in range(4):
            GP(lambda e, f=fi, ep=ep: e.dma_start(
                out=w2_prep[ep][f], in_=stgb[:, ep * 256:(ep + 1) * 256]), "qps")

    for ei in range(EB):
        prep_load(w_op[ei * P:(ei + 1) * P, :], DI)
        W("tensor", "sv", C("sv"))
        for half in range(2):
            W("tensor", "ss", C("ss"))
            for j in range(8):
                T(lambda e, j=j, di=half * 8 + j: e.transpose(
                    PSB[:, j * P:(j + 1) * P], stgb[:, di * P:(di + 1) * P], idb[:]),
                  "stt" if j == 7 else None)
            W("scalar", "stt", C("stt"))
            W("scalar", "qps", C("qps"))
            S(lambda e: e.activation(stg2[:], PSB[:], AF.Identity))
            W("gpsimd", "ss", C("ss"))
            for j in range(8):
                GP(lambda e, j=j, di=half * 8 + j, ep=ei // 2, c=(ei % 2) * P: e.dma_start(
                    out=wo_prep[ep][di][:, c:c + P], in_=stg2[:, j * P:(j + 1) * P]), "qps")

    prep_load(w_xp, DI, rows=96)
    W("tensor", "sv", C("sv"))
    for di in range(DB):
        W("tensor", "ss", C("ss"))
        T(lambda e, di=di: e.transpose(PSB[:, 0:96],
                                       stgb[0:96, di * P:(di + 1) * P],
                                       idb[0:96, 0:96]))
        W("scalar", "stt", C("stt"))
        S(lambda e, di=di: e.activation(wx[:, di * 96:(di + 1) * 96], PSB[:, 0:96], AF.Identity))

    for dblk in range(DB):
        W("sync", "sv", C("sv"))
        W("vector", "stt", C("stt"))
        SY(lambda e, d=dblk: e.dma_start(out=stg[:, 0:RNK], in_=w_dt[d * P:(d + 1) * P, :]), "qpw")
        W("vector", "qpw", C("qpw"))
        V(lambda e: e.tensor_copy(stgb[:, 0:RNK], stg[:, 0:RNK]))
        W("tensor", "sv", C("sv"))
        W("tensor", "ss", C("ss"))
        T(lambda e: e.transpose(PSB[0:RNK, 0:P], stgb[:, 0:RNK], idb[:]))
        W("scalar", "stt", C("stt"))
        S(lambda e, d=dblk: e.activation(wdt[:, d * P:(d + 1) * P], PSB[0:RNK, 0:P], AF.Identity))
    pl.mark("psb_free", "ss")

    W("sync", "qps", C("qps"))
    W("gpsimd", "qps", C("qps"))

    for ci in range(NCH_OVERRIDE or NCH):
        seq = ci // CPS
        l0 = (ci % CPS) * TC
        first = (ci % CPS) == 0

        for tb in range(TB):
            sl = tb % 2
            need("sync", ("xtfree", sl))
            sxs = f"sx{sl}"
            SY(lambda e, s=seq, l=l0 + tb * P, sl=sl: e.dma_start(
                out=xt[sl][:], in_=x_in[s, l:l + P, :]), sxs)
            qv = C(sxs)
            W("scalar", sxs, qv)
            need("scalar", "sq_free")
            S(lambda e, sl=sl: e.activation(sq[:], xt[sl][:], AF.Square))
            W("vector", sxs, qv)
            V(lambda e, sl=sl: e.reduce_sum(statc[:, 0:1], xt[sl][:], axis=AX.X))
            W("vector", "ss", C("ss"))
            V(lambda e: e.reduce_sum(statc[:, 1:2], sq[:], axis=AX.X))
            mark("sq_free", "sv")
            V(lambda e: e.drain(), inc=None)
            V(lambda e: e.tensor_scalar(statc[:, 2:3], statc[:, 0:1], 1.0 / DM, None, ALU.mult))
            V(lambda e: e.tensor_scalar(statc[:, 3:4], statc[:, 1:2], 1.0 / DM, None, ALU.mult))
            V(lambda e: e.tensor_mul(statc[:, 4:5], statc[:, 2:3], statc[:, 2:3]))
            V(lambda e: e.drain(), inc=None)
            V(lambda e: e.tensor_sub(statc[:, 5:6], statc[:, 3:4], statc[:, 4:5]))
            W("scalar", "sv", C("sv"))
            S(lambda e: e.activation(statc[:, 6:7], statc[:, 5:6], AF.Sqrt, bias=epsc[:, 0:1]))
            W("vector", "ss", C("ss"))
            V(lambda e: e.reciprocal(statc[:, 7:8], statc[:, 6:7]))
            V(lambda e: e.drain(), inc=None)
            V(lambda e: e.tensor_mul(statc[:, 4:5], statc[:, 2:3], statc[:, 7:8]))
            V(lambda e: e.drain(), inc=None)
            V(lambda e: e.tensor_scalar(statc[:, 4:5], statc[:, 4:5], -1.0, None, ALU.mult))
            V(lambda e: e.drain(), inc=None)
            need("vector", ("xnt_free", tb))
            V(lambda e, tb=tb, sl=sl: e.tensor_scalar(
                xnt[tb][:], xt[sl][:], statc[:, 7:8], statc[:, 4:5], ALU.mult, ALU.add))
            mark(("xnt", tb), "sv")
            mark(("xtfree", sl), "sv")
            dump(ci == 0, dbg.get("dbg_stat") and dbg["dbg_stat"][tb], statc[:], "sv")
            dump(ci == 0, dbg.get("dbg_xnt") and dbg["dbg_xnt"][tb], xnt[tb][:], "sv")
        for dm in range(DMB):
            for tb in range(TB):
                need("tensor", ("xnt", tb))
            need("tensor", "psb_free")
            for tb in range(TB):
                T(lambda e, dm=dm, tb=tb: e.transpose(
                    PSB[:, tb * P:(tb + 1) * P], xnt[tb][:, dm * P:(dm + 1) * P], idb[:]),
                  "stt" if tb == TB - 1 else None)
            W("scalar", "stt", C("stt"))
            need("scalar", ("xnd_free", dm))
            S(lambda e, dm=dm: e.activation(xnd[dm][:], PSB[:, 0:TC], AF.Identity))
            mark(("xnd", dm), "ss")
            mark("psb_free", "ss")
            dump(ci == 0, dbg.get("dbg_xnd") and dbg["dbg_xnd"][dm], xnd[dm][:], "ss")
        for tb in range(TB):
            mark(("xnt_free", tb), "stt")

        for f in range(FB):
            need("sync", "w1b_free")
            SY(lambda e, f=f: e.dma_start(out=w1b[:], in_=w1_prep[f]), "sw1")
            W("tensor", "sw1", C("sw1"))
            for dm in range(DMB):
                need("tensor", ("xnd", dm))
            pslot = f % 2
            need("tensor", ("psab_free", pslot))
            for dm in range(DMB):
                T(lambda e, dm=dm, p=pslot: e.matmul(
                    psb(p), w1b[:, dm * P:(dm + 1) * P], xnd[dm][:],
                    start=(dm == 0), stop=(dm == DMB - 1)),
                  "stt" if dm == DMB - 1 else None)
            mark("w1b_free", "stt")
            W("scalar", "stt", C("stt"))
            hsl = f % 2
            need("scalar", ("hid_free", hsl))
            S(lambda e, f=f, p=pslot, h=hsl: e.activation(
                hid[h][:], psb(p), AF.Identity, bias=b1c[:, f:f + 1]))
            mark(("psab_free", pslot), "ss")
            need("scalar", "s12_free")
            S(lambda e, f=f, h=hsl: e.activation(
                s1b, hid[h][:], AF.Identity, bias=thc[:, f:f + 1], scale=0.1))
            S(lambda e, f=f, h=hsl: e.activation(
                s2b, hid[h][:], AF.Identity, bias=thd[:, f:f + 1], scale=0.1 * INV_2PI))
            W("vector", "ss", C("ss"))
            need("vector", "qfb_free")
            V(lambda e: e.tensor_copy(qib[:], s2b))
            V(lambda e: e.tensor_copy(qfb[:], qib[:]))
            V(lambda e: e.scalar_tensor_tensor(snb[:], qfb[:], -TWO_PI, s1b, ALU.mult, ALU.add))
            W("scalar", "sv", C("sv"))
            S(lambda e: e.activation(snb[:], snb[:], AF.Sin))
            W("vector", "ss", C("ss"))
            V(lambda e, f=f, h=hsl: e.scalar_tensor_tensor(
                qfb[:], snb[:], kqc[:, f:f + 1], hid[h][:], ALU.mult, ALU.add))
            mark(("hid_free", hsl), "sv")
            mark("s12_free", "sv")
            W("scalar", "sv", C("sv"))
            need("scalar", ("act_free", f))
            S(lambda e, f=f: e.activation(act[f][:], qfb[:], AF.Relu))
            mark(("act", f), "ss")
            mark("qfb_free", "ss")
            if f == 0:
                dump(ci == 0, dbg.get("dbg_hid") and dbg["dbg_hid"], hid[0][:], "ss")
            dump(ci == 0, dbg.get("dbg_act") and dbg["dbg_act"][f], act[f][:], "ss")

        if first:
            for dblk in range(DB):
                need("vector", ("xpre_free", dblk))
                V(lambda e, d=dblk: e.memset(xpre[d][:, 0:KC - 1], 0.0))
                mark(("xpre_hist", dblk), "sv")
        for o in range(OB):
            need("sync", "wib_free")
            SY(lambda e, o=o: e.dma_start(out=wib[:], in_=wi_prep[o]), "swi")
            W("tensor", "swi", C("swi"))
            pslot = o % 2
            need("tensor", ("psab_free", pslot))
            for dm in range(DMB):
                T(lambda e, dm=dm, p=pslot: e.matmul(
                    psb(p), wib[:, dm * P:(dm + 1) * P], xnd[dm][:],
                    start=(dm == 0), stop=(dm == DMB - 1)),
                  "stt" if dm == DMB - 1 else None)
            mark("wib_free", "stt")
            if o == OB - 1:
                for dm in range(DMB):
                    mark(("xnd_free", dm), "stt")
            W("scalar", "stt", C("stt"))
            if o < DB:
                dblk = o
                need("scalar", ("xpre_free", dblk))
                need("scalar", ("xpre_hist", dblk))
                S(lambda e, d=dblk, p=pslot: e.activation(
                    xpre[d][:, KC - 1:KC - 1 + TC], psb(p), AF.Identity))
                mark(("xpre", dblk), "ss")
            else:
                need("scalar", "szt_free")
                S(lambda e, p=pslot: e.activation(szt[:], psb(p), AF.Silu))
                W("gpsimd", "ss", C("ss"))
                GP(lambda e, d=o - DB, ci=ci: e.dma_start(
                    out=sz_sp[d][:, ci * TC:(ci + 1) * TC], in_=szt[:]), "sszt")
                mark("szt_free", "sszt")
            mark(("psab_free", pslot), "ss")

        for dblk in range(DB):
            need("sync", "cvb_free")
            SY(lambda e, d=dblk: e.dma_start(out=cvb[:], in_=cv_prep[d]), "scv")
            W("tensor", "scv", C("scv"))
            need("tensor", ("xpre", dblk))
            pslot = 2 + (dblk % 2)
            need("tensor", ("pscd_free", pslot))
            need("tensor", ("ybank_free", pslot - 2))
            for k in range(KC):
                T(lambda e, k=k, d=dblk, p=pslot: e.matmul(
                    psb(p), cvb[:, k * P:(k + 1) * P], xpre[d][:, k:k + TC],
                    start=(k == 0), stop=(k == KC - 1)),
                  "stt" if k == KC - 1 else None)
            mark("cvb_free", "stt")
            mmv = C("stt")
            W("vector", "stt", mmv)
            V(lambda e, d=dblk: e.tensor_copy(xpre[d][:, 0:KC - 1],
                                              xpre[d][:, TC:TC + KC - 1]))
            mark(("xpre_free", dblk), "sv")
            mark(("xpre_hist", dblk), "sv")
            W("scalar", "stt", mmv)
            need("scalar", "xstg_free")
            S(lambda e, d=dblk, p=pslot: e.activation(
                xstg[:], psb(p), AF.Silu, bias=cbc[:, d:d + 1]))
            mark(("pscd_free", pslot), "ss")
            ssv = C("ss")
            W("gpsimd", "ss", ssv)
            GP(lambda e, d=dblk, ci=ci: e.dma_start(
                out=xs_sp[d][:, ci * TC:(ci + 1) * TC], in_=xstg[:]), "sxst")
            need("tensor", "ps4_free")
            need("tensor", ("ybank_free", 2))
            W("tensor", "ss", ssv)
            T(lambda e, d=dblk: e.matmul(psb(4)[0:96, :], wx[:, d * 96:(d + 1) * 96],
                                         xstg[:], start=(d == 0), stop=(d == DB - 1)))
            mark("xstg_free", "stt")
            amark("xstg_free", "sxst")
        W("scalar", "stt", C("stt"))
        need("scalar", "xdbl_free")
        S(lambda e: e.activation(xdbl[0:96, :], psb(4)[0:96, :], AF.Identity))
        mark("ps4_free", "ss")
        mark("xdbl", "ss")
        W("gpsimd", "ss", C("ss"))
        GP(lambda e, ci=ci: e.dma_start(out=bc_sp[ci], in_=xdbl[64:96, :]), "sbcs")
        mark("bcsp", "sbcs")
        dump(ci == 0, dbg.get("dbg_xdbl") and dbg["dbg_xdbl"], xdbl[:], "ss")
        for j in range(DBQ):
            amark(("ybank_free", j), "ss")

        for q in range(NQ):
            for j in range(DBQ):
                dblk = q * DBQ + j
                pslot = j % 2
                need("tensor", ("psab_free", pslot))
                need("tensor", "xdbl")
                T(lambda e, d=dblk, p=pslot: e.matmul(
                    psb(p), wdt[:, d * P:(d + 1) * P], xdbl[0:RNK, :],
                    start=True, stop=True))
                W("scalar", "stt", C("stt"))
                need("scalar", "sgb_free")
                S(lambda e, d=dblk, p=pslot: e.activation(
                    sgb[:], psb(p), AF.Sigmoid, bias=ndtb[:, d:d + 1], scale=-1.0))
                mark(("psab_free", pslot), "ss")
                need("scalar", ("ndl_free", j))
                S(lambda e, j=j: e.activation(ndl[j][:], sgb[:], AF.Ln))
                mark(("ndl", j), "ss")
                mark("sgb_free", "ss")
                need("sync", ("xsl_free", j))
                W("sync", "sxst", C("sxst"))
                SY(lambda e, d=dblk, ci=ci, j=j, sem=f"sxs{j}": e.dma_start(
                    out=xsl[j][:], in_=xs_sp[d][:, ci * TC:(ci + 1) * TC]), f"sxs{j}")
                mark(("xsl", j), f"sxs{j}")
                need("vector", ("ndl", j))
                need("vector", ("xsl", j))
                need("vector", ("wdb_free", j))
                V(lambda e, j=j: e.scalar_tensor_tensor(
                    wdb[j][:], ndl[j][:], -1.0, xsl[j][:], ALU.mult, ALU.mult))
                mark(("wdb", j), "sv")
                dump(ci == 0, dbg.get("dbg_ndl") and dbg["dbg_ndl"][dblk], ndl[j][:], "ss")
                dump(ci == 0, dbg.get("dbg_wdb") and dbg["dbg_wdb"][dblk], wdb[j][:], "sv")
                dump(ci == 0, dbg.get("dbg_xs") and dbg["dbg_xs"][dblk], xsl[j][:], "sv")
            for grp in range(NST // 2):
                bsl = grp % 2
                need("sync", ("brep_free", bsl))
                need("sync", "bcsp")
                for g in range(2):
                    n = grp * 2 + g
                    SY(lambda e, n=n, g=g, b=bsl, ci=ci: e.dma_start(
                        out=brep[b][:, g * TC:(g + 1) * TC],
                        in_=bass.AP(bc_sp.tensor, (ci * 32 + n) * TC,
                                    [[1, 1], [0, P], [1, TC]])), f"sbc{bsl}")
                    SY(lambda e, n=n, g=g, b=bsl, ci=ci: e.dma_start(
                        out=crep[b][:, g * TC:(g + 1) * TC],
                        in_=bass.AP(bc_sp.tensor, (ci * 32 + 16 + n) * TC,
                                    [[1, 1], [0, P], [1, TC]])), f"sbc{bsl}")
                mark(("brep", bsl), f"sbc{bsl}")
                for j in range(DBQ):
                    dblk = q * DBQ + j
                    for g in range(2):
                        n = grp * 2 + g
                        colx = dblk * NST + n
                        usl = g
                        need("scalar", ("abuf_free", usl))
                        S(lambda e, j=j, c=colx, u=usl: e.activation(
                            abuf[u][:], ndl[j][:], AF.Exp, scale=aexp[:, c:c + 1]))
                        mark(("abuf", usl), "ss")
                        need("vector", ("brep", bsl))
                        need("vector", ("wdb", j))
                        need("vector", ("ubuf_free", usl))
                        V(lambda e, j=j, g=g, b=bsl, u=usl: e.tensor_tensor(
                            ubuf[u][:], wdb[j][:], brep[b][:, g * TC:(g + 1) * TC], ALU.mult))
                        need("vector", ("abuf", usl))
                        if first:
                            V(lambda e, u=usl: e.tensor_tensor_scan(
                                hbuf[u][:], abuf[u][:], ubuf[u][:], 0.0, ALU.mult, ALU.add))
                        else:
                            V(lambda e, u=usl, c=colx: e.tensor_tensor_scan(
                                hbuf[u][:], abuf[u][:], ubuf[u][:], carry[:, c:c + 1],
                                ALU.mult, ALU.add))
                        mark(("abuf_free", usl), "sv")
                        V(lambda e, u=usl, c=colx: e.tensor_copy(
                            carry[:, c:c + 1], hbuf[u][:, TC - 1:TC]))
                        if q == 0 and j == 0 and grp == 0 and g == 0:
                            dump(ci == 0, dbg.get("dbg_a") and dbg["dbg_a"], abuf[usl][:], "sv")
                            dump(ci == 0, dbg.get("dbg_u") and dbg["dbg_u"], ubuf[usl][:], "sv")
                            dump(ci == 0, dbg.get("dbg_h") and dbg["dbg_h"], hbuf[usl][:], "sv")
                        need("vector", ("pbuf_free", usl))
                        V(lambda e, u=usl, g=g, b=bsl: e.tensor_tensor(
                            pbuf[u][:], hbuf[u][:], crep[b][:, g * TC:(g + 1) * TC], ALU.mult))
                        mark(("pbuf", usl), "sv")
                        mark(("ubuf_free", usl), "sv")
                        need("tensor", ("pbuf", usl))
                        if grp == 0 and g == 0:
                            need("tensor", ("ybank_free", j))
                        T(lambda e, j=j, u=usl, st=(grp == 0 and g == 0),
                          sp=(grp == NST // 2 - 1 and g == 1): e.matmul(
                            psb(2 + j), idb[:], pbuf[u][:], start=st, stop=sp))
                        mark(("pbuf_free", usl), "stt")
                mark(("brep_free", bsl), "stt")
            for j in range(DBQ):
                dblk = q * DBQ + j
                ssl = j % 2
                need("sync", ("szl_free", ssl))
                W("sync", "sszt", C("sszt"))
                SY(lambda e, d=dblk, ci=ci, s=ssl: e.dma_start(
                    out=szl[s][:], in_=sz_sp[d][:, ci * TC:(ci + 1) * TC]), f"ssz{ssl}")
                mark(("szl", ssl), f"ssz{ssl}")
                W("vector", "stt", C("stt"))
                need("vector", ("xsl", j))
                need("vector", "t1_free")
                V(lambda e, j=j, d=dblk: e.scalar_tensor_tensor(
                    t1b[:], xsl[j][:], dcol[:, d:d + 1], PS[:, (2 + j) * 512:(3 + j) * 512],
                    ALU.mult, ALU.add))
                mark(("xsl_free", j), "sv")
                mark(("ndl_free", j), "sv")
                mark(("wdb_free", j), "sv")
                mark(("ybank_free", j), "sv")
                dump(ci == 0, dbg.get("dbg_t1") and dbg["dbg_t1"][dblk], t1b[:], "sv")
                need("vector", ("szl", ssl))
                need("vector", ("ybf_free", dblk))
                V(lambda e, d=dblk, s=ssl: e.tensor_tensor(
                    ybf[d][:], t1b[:], szl[s][:], ALU.mult))
                mark(("ybf", dblk), "sv")
                dump(ci == 0, dbg.get("dbg_ybf") and dbg["dbg_ybf"][dblk], ybf[dblk][:], "sv")
                dump(ci == 0, dbg.get("dbg_sz") and dbg["dbg_sz"][dblk], szl[ssl][:], "sv")
                mark(("szl_free", ssl), "sv")
                mark("t1_free", "sv")

        for ep in range(4):
            need("tensor", ("psab_free", 0))
            need("tensor", ("psab_free", 1))
            for sub in range(4):
                need("sync", "w2b_free")
                SY(lambda e, ep=ep, sub=sub: e.dma_start(
                    out=w2b[:], in_=w2_prep[ep][sub * 8:(sub + 1) * 8].transpose([1, 0, 2])), "sw2")
                W("tensor", "sw2", C("sw2"))
                for f8 in range(8):
                    f = sub * 8 + f8
                    need("tensor", ("act", f))
                    for g in range(2):
                        T(lambda e, f=f, f8=f8, g=g, st=(sub == 0 and f8 == 0): e.matmul(
                            psb(g), w2b[:, f8 * 256 + g * P:f8 * 256 + (g + 1) * P],
                            act[f][:], start=st, stop=False))
                    if ep == 3:
                        mark(("act_free", f), "stt")
                mark("w2b_free", "stt")
            for sub in range(2):
                need("sync", "wob_free")
                SY(lambda e, ep=ep, sub=sub: e.dma_start(
                    out=wob[:], in_=wo_prep[ep][sub * 8:(sub + 1) * 8].transpose([1, 0, 2])), "swo")
                W("tensor", "swo", C("swo"))
                for d8 in range(8):
                    di = sub * 8 + d8
                    need("tensor", ("ybf", di))
                    for g in range(2):
                        T(lambda e, d8=d8, g=g, di=di,
                          sp=(sub == 1 and d8 == 7 and g == 1): e.matmul(
                            psb(g), wob[:, d8 * 256 + g * P:d8 * 256 + (g + 1) * P],
                            ybf[di][:], start=False, stop=sp))
                    if ep == 3:
                        mark(("ybf_free", di), "stt")
                mark("wob_free", "stt")
            W("scalar", "stt", C("stt"))
            for g in range(2):
                eidx = ep * 2 + g
                need("scalar", ("comb_free", eidx))
                S(lambda e, g=g, eidx=eidx: e.activation(
                    comb[eidx][:], psb(g), AF.Identity, bias=b2c[:, eidx:eidx + 1]))
                mark(("comb", eidx), "ss")
                dump(ci == 0, dbg.get("dbg_comb") and dbg["dbg_comb"][eidx], comb[eidx][:], "ss")
            mark(("psab_free", 0), "ss")
            mark(("psab_free", 1), "ss")

        for tb in range(TB):
            sl = tb % 2
            need("sync", ("xtfree", sl))
            sxs = f"sx{sl}"
            SY(lambda e, s=seq, l=l0 + tb * P, sl=sl: e.dma_start(
                out=xt[sl][:], in_=x_in[s, l:l + P, :]), sxs)
            qv = C(sxs)
            need("tensor", "psb_free")
            for e8 in range(EB):
                need("tensor", ("comb", e8))
                T(lambda e, e8=e8, tb=tb: e.transpose(
                    PSB[:, e8 * P:(e8 + 1) * P], comb[e8][:, tb * P:(tb + 1) * P], idb[:]),
                  "stt" if e8 == EB - 1 else None)
            W("vector", "stt", C("stt"))
            W("vector", sxs, qv)
            need("vector", ("oute_free", sl))
            V(lambda e, sl=sl: e.tensor_tensor(oute[sl][:], PSB[:], xt[sl][:], ALU.add))
            mark("psb_free", "sv")
            mark(("xtfree", sl), "sv")
            if tb == TB - 1:
                for e8 in range(EB):
                    mark(("comb_free", e8), "stt")
            W("gpsimd", "sv", C("sv"))
            GP(lambda e, s=seq, l=l0 + tb * P, sl=sl: e.dma_start(
                out=out[s, l:l + P, :], in_=oute[sl][:]), f"so{sl}")
            mark(("oute_free", sl), f"so{sl}")

    W("gpsimd", "so0", C("so0"))
    W("gpsimd", "so1", C("so1"))
    GP(lambda e: e.nop())

    pl.check()
    block = stack.enter_context(nc.Block())
    pl.run(block)
    stack.close()
    return nc


_NC_CACHE = {}


def kernel(**inputs):
    x = np.ascontiguousarray(inputs["L0_attn_hidden_states"])
    wmap = {
        "w_ip": inputs["L0_attn_in_proj_w"],
        "w_cv": inputs["L0_attn_conv_w"],
        "b_cv": inputs["L0_attn_conv_b"],
        "w_xp": inputs["L0_attn_x_proj_w"],
        "w_dt": inputs["L0_attn_dt_proj_w"],
        "b_dt": inputs["L0_attn_dt_proj_b"],
        "a_log": inputs["L0_attn_A_log"],
        "d_prm": inputs["L0_attn_D_param"],
        "w_op": inputs["L0_attn_out_proj_w"],
        "w_1": inputs["L0_ffn_W1"],
        "b_1": inputs["L0_ffn_b1"],
        "q_ph": inputs["L0_ffn_quantum_phases"],
        "q_wr": inputs["L0_ffn_quantum_weights_real"],
        "q_wi": inputs["L0_ffn_quantum_weights_imag"],
        "w_2": inputs["L0_ffn_W2"],
        "b_2": inputs["L0_ffn_b2"],
    }
    wmap = {k: np.ascontiguousarray(np.asarray(v, dtype=np.float32)) for k, v in wmap.items()}
    if "nc" not in _NC_CACHE:
        _NC_CACHE["nc"] = build_program()
    nc = _NC_CACHE["nc"]
    in_maps = []
    for c in range(8):
        m = dict(wmap)
        m["x"] = x[2 * c:2 * c + 2]
        in_maps.append(m)
    res = run_bass_kernel_spmd(nc, in_maps, core_ids=list(range(8)))
    return np.concatenate([np.asarray(r["out"], dtype=np.float32) for r in res.results], axis=0)


if __name__ == "__main__":
    import json
    rng = np.random.RandomState(0)
    print("build test")
    nc = build_program()
    print("built ok")
